# version 02 (final)
# baseline (speedup 1.0000x reference)
import numpy as np
import concourse.bass as bass
import concourse.mybir as mybir
from concourse.bass_utils import run_bass_kernel_spmd

F32 = mybir.dt.float32
I32 = mybir.dt.int32
AF = mybir.ActivationFunctionType
OP = mybir.AluOpType
AX = mybir.AxisListType

L = 4096
D = 1024
DI = 2048
CONVD = 2176
FIN = 4352
HID = 3072
Q = 128
NH = 32
HD = 64


def build():
    nc = bass.Bass("TRN2", target_bir_lowering=False)
    dt = nc.dram_tensor
    x = dt("x", [L, D], F32, kind="ExternalInput").ap()
    WaT = dt("WaT", [D, FIN], F32, kind="ExternalInput").ap()
    acw = dt("acw", [128, 68], F32, kind="ExternalInput").ap()
    acb = dt("acb", [128, 17], F32, kind="ExternalInput").ap()
    dtbc = dt("dtbc", [32, 1], F32, kind="ExternalInput").ap()
    Aac = dt("Aac", [32, 1], F32, kind="ExternalInput").ap()
    Dsk = dt("Dsk", [128, 16], F32, kind="ExternalInput").ap()
    nrm = dt("nrm", [128, 16], F32, kind="ExternalInput").ap()
    WoT = dt("WoT", [DI, D], F32, kind="ExternalInput").ap()
    lnw = dt("lnw", [128, D], F32, kind="ExternalInput").ap()
    lnb = dt("lnb", [128, D], F32, kind="ExternalInput").ap()
    wgi = dt("wgi", [D, HID], I32, kind="ExternalInput").ap()
    wui = dt("wui", [D, HID], I32, kind="ExternalInput").ap()
    wdi = dt("wdi", [HID, D], I32, kind="ExternalInput").ap()
    Wbin = dt("Wbin", [D, 2 * DI], F32, kind="ExternalInput").ap()
    bcw = dt("bcw", [128, 48], F32, kind="ExternalInput").ap()
    bcb = dt("bcb", [128, 16], F32, kind="ExternalInput").ap()
    Wst = dt("Wst", [DI, 32], F32, kind="ExternalInput").ap()
    Wtm = dt("Wtm", [DI, DI], F32, kind="ExternalInput").ap()
    btm = dt("btm", [128, 16], F32, kind="ExternalInput").ap()
    Ab2 = dt("Ab2", [128, 256], F32, kind="ExternalInput").ap()
    skipc = dt("skipc", [128, 16], F32, kind="ExternalInput").ap()
    WbO = dt("WbO", [DI, D], F32, kind="ExternalInput").ap()
    id128 = dt("id128", [128, 128], F32, kind="ExternalInput").ap()
    triu = dt("triu", [128, 128], F32, kind="ExternalInput").ap()
    out = dt("out", [L, D], F32, kind="ExternalOutput").ap()
    r_ln1 = dt("r_ln1", [L, D], F32, kind="Internal").ap()
    z_cm = dt("z_cm", [DI, L], F32, kind="Internal").ap()
    xbc_pad = dt("xbc_pad", [CONVD, L + 3], F32, kind="Internal").ap()
    xbcs = dt("xbcs", [CONVD, L], F32, kind="Internal").ap()
    dtraw = dt("dtraw", [128, L], F32, kind="Internal").ap()
    acum_scr = dt("acum_scr", [32, 128], F32, kind="Internal").ap()
    ygn_cm = dt("ygn_cm", [DI, L], F32, kind="Internal").ap()
    x1 = dt("x1", [L, D], F32, kind="Internal").ap()
    x2 = dt("x2", [L, D], F32, kind="Internal").ap()
    x3 = dt("x3", [L, D], F32, kind="Internal").ap()
    wgf = dt("wgf", [D, HID], F32, kind="Internal").ap()
    wuf = dt("wuf", [D, HID], F32, kind="Internal").ap()
    wdf = dt("wdf", [HID, D], F32, kind="Internal").ap()
    inpad = dt("inpad", [DI, L + 2], F32, kind="Internal").ap()
    gate_b = dt("gate_b", [DI, L], F32, kind="Internal").ap()
    hb_cm = dt("hb_cm", [DI, L], F32, kind="Internal").ap()
    dtb_cm = dt("dtb_cm", [DI, L], F32, kind="Internal").ap()
    Bcm2 = dt("Bcm2", [16, L], F32, kind="Internal").ap()
    ysg_cm = dt("ysg_cm", [DI, L], F32, kind="Internal").ap()

    sched = []
    ctr = [0]
    last_eng = [None]

    def op(eng, inc, fn):
        wv = ctr[0] if eng != last_eng[0] else -1
        sched.append((eng, wv, inc, fn))
        ctr[0] += inc
        last_eng[0] = eng

    def dma(o, i):
        op("sync", 16, lambda e, o=o, i=i: e.dma_start(out=o, in_=i, allow_slow_non_contiguous=True))

    def mm(o, lhsT, rhs, st, sp):
        op("tensor", 1, lambda e, o=o, l=lhsT, r=rhs, st=st, sp=sp: e.matmul(out=o, lhsT=l, rhs=r, start=st, stop=sp))

    def tp(o, i, idt):
        op("tensor", 1, lambda e, o=o, i=i, d=idt: e.transpose(out=o, in_=i, identity=d))

    def tt(o, a, b, alu):
        op("vector", 1, lambda e, o=o, a=a, b=b, al=alu: e.tensor_tensor(out=o, in0=a, in1=b, op=al))

    def ts(o, a, s, alu):
        op("vector", 1, lambda e, o=o, a=a, s=s, al=alu: e.tensor_scalar(out=o, in0=a, scalar1=s, scalar2=None, op0=al))

    def act(o, i, f, bias=0.0, scale=1.0):
        op("scalar", 1, lambda e, o=o, i=i, f=f, b=bias, s=scale: e.activation(out=o, in_=i, func=f, bias=b, scale=s))

    def red(o, i, alu):
        op("vector", 1, lambda e, o=o, i=i, al=alu: e.tensor_reduce(out=o, in_=i, axis=AX.X, op=alu))

    def mset(a, c):
        op("vector", 1, lambda e, a=a, c=c: e.memset(a, c))

    def cpy(o, i):
        op("vector", 1, lambda e, o=o, i=i: e.tensor_copy(out=o, in_=i))

    import contextlib
    with contextlib.ExitStack() as stack:
        ec = stack.enter_context
        BIG = ec(nc.sbuf_tensor("BIG", [128, 16384], F32))
        WA = ec(nc.sbuf_tensor("WA", [128, 8192], F32))
        SM = ec(nc.sbuf_tensor("SM", [128, 3072], F32))
        SMI = ec(nc.sbuf_tensor("SMI", [128, 3072], I32))
        dt_sb = ec(nc.sbuf_tensor("dt_sb", [32, L], F32))
        adt_sb = ec(nc.sbuf_tensor("adt_sb", [32, L], F32))
        rowcat = ec(nc.sbuf_tensor("rowcat", [1, L], F32))
        negrow = ec(nc.sbuf_tensor("negrow", [1, L], F32))
        lnw_sb = ec(nc.sbuf_tensor("lnw_sb", [128, D], F32))
        lnb_sb = ec(nc.sbuf_tensor("lnb_sb", [128, D], F32))
        id_sb = ec(nc.sbuf_tensor("id_sb", [128, 128], F32))
        triu_sb = ec(nc.sbuf_tensor("triu_sb", [128, 128], F32))
        ones1r = ec(nc.sbuf_tensor("ones1r", [1, 128], F32))
        onescol = ec(nc.sbuf_tensor("onescol", [128, 1], F32))
        S_sb = ec(nc.sbuf_tensor("S_sb", [64, 2048], F32))
        Sb_sb = ec(nc.sbuf_tensor("Sb_sb", [128, 256], F32))
        cols = ec(nc.sbuf_tensor("cols", [128, 256], F32))
        tiny = ec(nc.sbuf_tensor("tiny", [32, 8], F32))
        U = ec(nc.sbuf_tensor("U", [128, 1024], F32))
        P0 = ec(nc.psum_tensor("P0", [128, 512], F32))
        P1 = ec(nc.psum_tensor("P1", [128, 512], F32))
        P2 = ec(nc.psum_tensor("P2", [128, 512], F32))
        P3 = ec(nc.psum_tensor("P3", [128, 512], F32))
        P4 = ec(nc.psum_tensor("P4", [128, 512], F32))
        P5 = ec(nc.psum_tensor("P5", [128, 512], F32))
        P6 = ec(nc.psum_tensor("P6", [128, 512], F32))
        P7 = ec(nc.psum_tensor("P7", [128, 512], F32))
        g = ec(nc.semaphore("g"))
        blk = ec(nc.Block())
        BIG = BIG[:, :]; WA = WA[:, :]; SM = SM[:, :]; SMI = SMI[:, :]
        dt_sb = dt_sb[:, :]; adt_sb = adt_sb[:, :]; rowcat = rowcat[:, :]
        negrow = negrow[:, :]; lnw_sb = lnw_sb[:, :]; lnb_sb = lnb_sb[:, :]
        id_sb = id_sb[:, :]; triu_sb = triu_sb[:, :]; ones1r = ones1r[:, :]
        onescol = onescol[:, :]; S_sb = S_sb[:, :]; Sb_sb = Sb_sb[:, :]
        cols = cols[:, :]; tiny = tiny[:, :]; U = U[:, :]
        P0 = P0[:, :]; P1 = P1[:, :]; P2 = P2[:, :]; P3 = P3[:, :]
        P4 = P4[:, :]; P5 = P5[:, :]; P6 = P6[:, :]; P7 = P7[:, :]
        constvals = [0.0, 1.0, 1e-5, -1.0, -0.5, -1.0 / 1024.0, 1.0 / 1024.0,
                     1.0 / 2048.0, 1.0 / 127.0]
        for ci, cv in enumerate(constvals):
            mset(U[:, 512 + ci:513 + ci], cv)
            nc.const_aps.register(F32, cv, U[:, 512 + ci:513 + ci])
        dma(id_sb[:, :], id128[:, :])
        dma(triu_sb[:, :], triu[:, :])
        dma(lnw_sb[:, :], lnw[:, :])
        dma(lnb_sb[:, :], lnb[:, :])
        mset(ones1r[:, :], 1.0)
        mset(onescol[:, :], 1.0)
        mset(S_sb[:, :], 0.0)
        mset(Sb_sb[:, :], 0.0)
        mset(cols[:, :], 0.0)
        for ct in range(17):
            dma(xbc_pad[ct * 128:(ct + 1) * 128, 0:3], cols[:, 0:3])
        for ct in range(16):
            dma(inpad[ct * 128:(ct + 1) * 128, 0:1], cols[:, 0:1])
            dma(inpad[ct * 128:(ct + 1) * 128, L + 1:L + 2], cols[:, 0:1])
        dma(cols[:, 0:68], acw[:, :])
        dma(cols[:, 68:85], acb[:, :])
        dma(cols[:, 85:101], Dsk[:, :])
        dma(cols[:, 101:117], nrm[:, :])
        dma(cols[:, 117:133], bcb[:, :])
        dma(cols[:, 133:181], bcw[:, :])
        dma(cols[:, 181:197], btm[:, :])
        dma(cols[:, 197:213], skipc[:, :])
        dma(tiny[:, 0:1], dtbc[:, :])
        dma(tiny[:, 1:2], Aac[:, :])
        act(tiny[:, 2:3], tiny[:, 1:2], AF.Exp)
        act(tiny[:, 3:4], tiny[:, 2:3], AF.Copy, scale=-1.0)

        def softplus(dst, src, tmp1, tmp2):
            act(tmp1, src, AF.Abs)
            act(tmp1, tmp1, AF.Exp, scale=-1.0)
            act(tmp1, tmp1, AF.Ln, bias=1.0)
            act(tmp2, src, AF.Relu)
            tt(dst, tmp1, tmp2, OP.add)

        def layer_norm_pass(src, dst_arena_off, slab, affine, save_ln1):
            for tti in range(4):
                t0 = slab * 512 + tti * 128
                xt = SM[:, 0:1024]
                dma(xt, src[t0:t0 + 128, :])
                mu = U[:, 0:1]
                red(mu, xt, OP.add)
                act(mu, mu, AF.Copy, scale=-1.0 / 1024.0)
                ts(xt, xt, mu, OP.add)
                sq = SM[:, 1024:2048]
                act(sq, xt, AF.Square)
                v = U[:, 1:2]
                red(v, sq, OP.add)
                act(v, v, AF.Ln, bias=1e-5, scale=1.0 / 1024.0)
                act(v, v, AF.Exp, scale=-0.5)
                ts(xt, xt, v, OP.mult)
                if save_ln1 is not None:
                    dma(save_ln1[t0:t0 + 128, :], xt)
                if affine:
                    red(mu, xt, OP.add)
                    act(mu, mu, AF.Copy, scale=-1.0 / 1024.0)
                    ts(xt, xt, mu, OP.add)
                    act(sq, xt, AF.Square)
                    red(v, sq, OP.add)
                    act(v, v, AF.Ln, bias=1e-5, scale=1.0 / 1024.0)
                    act(v, v, AF.Exp, scale=-0.5)
                    ts(xt, xt, v, OP.mult)
                    tt(xt, xt, lnw_sb, OP.mult)
                    tt(xt, xt, lnb_sb, OP.add)
                for kt in range(8):
                    tp(P4[:, 0:128], xt[:, kt * 128:(kt + 1) * 128], id_sb)
                    cpy(BIG[:, dst_arena_off + kt * 512 + tti * 128: dst_arena_off + kt * 512 + (tti + 1) * 128], P4[:, 0:128])

        XO = 0
        for slab in range(8):
            layer_norm_pass(x, XO, slab, True, r_ln1)
            for ft in range(34):
                for kt in range(8):
                    wtile = WA[:, 0:128]
                    dma(wtile, WaT[kt * 128:(kt + 1) * 128, ft * 128:(ft + 1) * 128])
                    mm(P0[:, :], wtile, BIG[:, XO + kt * 512:XO + (kt + 1) * 512], kt == 0, kt == 7)
                zxt = SM[:, 0:512]
                cpy(zxt, P0[:, :])
                c0 = slab * 512
                if ft < 16:
                    dma(z_cm[ft * 128:(ft + 1) * 128, c0:c0 + 512], zxt)
                elif ft < 33:
                    ct = ft - 16
                    dma(xbc_pad[ct * 128:(ct + 1) * 128, c0 + 3:c0 + 515], zxt)
                else:
                    dma(dtraw[:, c0:c0 + 512], zxt)
        for ct in range(17):
            for slab in range(8):
                c0 = slab * 512
                xin = SM[:, 0:515]
                dma(xin, xbc_pad[ct * 128:(ct + 1) * 128, c0:c0 + 515])
                accv = SM[:, 1024:1536]
                tmpv = SM[:, 1536:2048]
                ts(accv, xin[:, 0:512], cols[:, ct * 4 + 0:ct * 4 + 1], OP.mult)
                for j in range(1, 4):
                    ts(tmpv, xin[:, j:j + 512], cols[:, ct * 4 + j:ct * 4 + j + 1], OP.mult)
                    tt(accv, accv, tmpv, OP.add)
                ts(accv, accv, cols[:, 68 + ct:68 + ct + 1], OP.add)
                act(accv, accv, AF.Silu)
                dma(xbcs[ct * 128:(ct + 1) * 128, c0:c0 + 512], accv)
        dma(dt_sb[:, :], dtraw[0:32, :])
        ts(dt_sb, dt_sb, tiny[0:32, 0:1], OP.add)
        softplus(dt_sb, dt_sb, adt_sb, BIG[0:32, 0:L])
        ts(adt_sb, dt_sb, tiny[0:32, 3:4], OP.mult)
        XD = 0
        YC = 2048
        SKX = 4096
        XB = 6144
        MT = 8320
        BCT = 8448
        BTM = 8576
        DTT = 8640
        ACU = 8672
        ACM = 8704
        SEG = 8832
        T1 = 8960
        T2 = 9024
        DEC = 9088
        EAC = 9089
        DECS = 9090
        for cq in range(32):
            c0 = cq * 128
            for ct in range(17):
                dma(BIG[:, XB + ct * 128:XB + (ct + 1) * 128], xbcs[ct * 128:(ct + 1) * 128, c0:c0 + 128])
            tp(P4[:, 0:32], adt_sb[:, c0:c0 + 128], id_sb[0:32, 0:32])
            cpy(BIG[:, MT:MT + 32], P4[:, 0:32])
            mm(P5[:, 0:32], triu_sb, BIG[:, MT:MT + 32], True, True)
            cpy(BIG[:, ACU:ACU + 32], P5[:, 0:32])
            tp(P4[0:32, 0:128], BIG[:, ACU:ACU + 32], id_sb)
            cpy(BIG[0:32, ACM:ACM + 128], P4[0:32, 0:128])
            dma(acum_scr[:, :], BIG[0:32, ACM:ACM + 128])
            dma(rowcat[:, :], acum_scr.rearrange("h l -> () (h l)"))
            act(negrow[:, :], rowcat[:, :], AF.Copy, scale=-1.0)
            tp(P4[:, 0:32], dt_sb[:, c0:c0 + 128], id_sb[0:32, 0:32])
            cpy(BIG[:, DTT:DTT + 32], P4[:, 0:32])
            for pt in range(16):
                tp(P4[:, 0:128], BIG[:, XB + pt * 128:XB + (pt + 1) * 128], id_sb)
                ts(BIG[:, XD + pt * 128:XD + pt * 128 + 64], P4[:, 0:64], BIG[:, DTT + 2 * pt:DTT + 2 * pt + 1], OP.mult)
                ts(BIG[:, XD + pt * 128 + 64:XD + (pt + 1) * 128], P4[:, 64:128], BIG[:, DTT + 2 * pt + 1:DTT + 2 * pt + 2], OP.mult)
                ts(BIG[:, SKX + pt * 128:SKX + (pt + 1) * 128], BIG[:, XB + pt * 128:XB + (pt + 1) * 128], cols[:, 85 + pt:86 + pt], OP.mult)
            Bcm = BIG[0:64, XB + 16 * 128:XB + 17 * 128]
            Ccm = BIG[64:128, XB + 16 * 128:XB + 17 * 128]
            CC = 9216
            cpy(BIG[0:64, CC:CC + 128], Ccm)
            Ccp = BIG[0:64, CC:CC + 128]
            mm(P5[:, 0:128], Bcm, Ccp, True, True)
            tt(BIG[:, BCT:BCT + 128], P5[:, 0:128], triu_sb, OP.mult)
            tp(P4[:, 0:64], Bcm, id_sb[0:64, 0:64])
            cpy(BIG[:, BTM:BTM + 64], P4[:, 0:64])
            for h in range(NH):
                rch = rowcat[:, h * 128:(h + 1) * 128]
                nrh = negrow[:, h * 128:(h + 1) * 128]
                mm(P4[:, 0:128], ones1r, rch, True, False)
                mm(P4[:, 0:128], nrh, ones1r, False, True)
                ts(BIG[:, SEG:SEG + 128], P4[:, 0:128], 0.0, OP.min)
                act(BIG[:, SEG:SEG + 128], BIG[:, SEG:SEG + 128], AF.Exp)
                tt(BIG[:, MT:MT + 128], BIG[:, SEG:SEG + 128], BIG[:, BCT:BCT + 128], OP.mult)
                mm(P5[:, 0:64], BIG[:, MT:MT + 128], BIG[:, XD + h * 64:XD + (h + 1) * 64], True, True)
                mm(P6[:, 0:1], nrh, ones1r[:, 0:1], True, False)
                mm(P6[:, 0:1], ones1r, rowcat[:, h * 128 + 127:h * 128 + 128], False, True)
                act(BIG[:, DEC:DEC + 1], P6[:, 0:1], AF.Exp)
                ts(BIG[:, T1:T1 + 64], BIG[:, BTM:BTM + 64], BIG[:, DEC:DEC + 1], OP.mult)
                mm(P6[0:64, 0:64], BIG[:, T1:T1 + 64], BIG[:, XD + h * 64:XD + (h + 1) * 64], True, True)
                Sh = S_sb[0:64, h * 64:(h + 1) * 64]
                mm(P7[:, 0:64], Ccp, Sh, True, True)
                act(BIG[:, EAC:EAC + 1], BIG[:, ACU + h:ACU + h + 1], AF.Exp)
                ts(BIG[:, T1:T1 + 64], P7[:, 0:64], BIG[:, EAC:EAC + 1], OP.mult)
                op("vector", 1, lambda e, h=h: e.tensor_tensor(
                    out=BIG[:, T1:T1 + 64], in0=BIG[:, T1:T1 + 64], in1=P5[:, 0:64], op=OP.add))
                tp(P4[0:64, 0:128], BIG[:, T1:T1 + 64], id_sb)
                pt = h // 2
                r0 = (h % 2) * 64
                tt(BIG[r0:r0 + 64, YC + pt * 128:YC + (pt + 1) * 128], P4[0:64, 0:128],
                   BIG[r0:r0 + 64, SKX + pt * 128:SKX + (pt + 1) * 128], OP.add)
                act(BIG[0:1, DECS:DECS + 1], rowcat[:, h * 128 + 127:h * 128 + 128], AF.Exp)
                mm(P7[0:64, 64:65], ones1r[:, 0:64], BIG[0:1, DECS:DECS + 1], True, True)
                op("scalar", 1, lambda e: e.copy(out=BIG[0:64, DEC:DEC + 1], in_=P7[0:64, 64:65]))
                ts(BIG[0:64, T2:T2 + 64], Sh, BIG[0:64, DEC:DEC + 1], OP.mult)
                tt(Sh, BIG[0:64, T2:T2 + 64], P6[0:64, 0:64], OP.add)
            for pt in range(16):
                zt = BIG[:, XB + pt * 128:XB + (pt + 1) * 128]
                dma(zt, z_cm[pt * 128:(pt + 1) * 128, c0:c0 + 128])
                act(zt, zt, AF.Silu)
                tt(BIG[:, XD + pt * 128:XD + (pt + 1) * 128], BIG[:, YC + pt * 128:YC + (pt + 1) * 128], zt, OP.mult)
                act(BIG[:, SKX + pt * 128:SKX + (pt + 1) * 128], BIG[:, XD + pt * 128:XD + (pt + 1) * 128], AF.Square)
                mm(P7[0:1, 0:128], onescol, BIG[:, SKX + pt * 128:SKX + (pt + 1) * 128], pt == 0, pt == 15)
            act(BIG[0:1, SEG:SEG + 128], P7[0:1, 0:128], AF.Ln, bias=1e-5, scale=1.0 / 2048.0)
            act(BIG[0:1, SEG:SEG + 128], BIG[0:1, SEG:SEG + 128], AF.Exp, scale=-0.5)
            mm(P6[:, 0:128], ones1r, BIG[0:1, SEG:SEG + 128], True, True)
            for pt in range(16):
                tt(BIG[:, XD + pt * 128:XD + (pt + 1) * 128], BIG[:, XD + pt * 128:XD + (pt + 1) * 128], P6[:, 0:128], OP.mult)
                ts(BIG[:, XD + pt * 128:XD + (pt + 1) * 128], BIG[:, XD + pt * 128:XD + (pt + 1) * 128], cols[:, 101 + pt:102 + pt], OP.mult)
                dma(ygn_cm[pt * 128:(pt + 1) * 128, c0:c0 + 128], BIG[:, XD + pt * 128:XD + (pt + 1) * 128])
        for slab in range(8):
            c0 = slab * 512
            for pt in range(16):
                dma(BIG[:, pt * 512:(pt + 1) * 512], ygn_cm[pt * 128:(pt + 1) * 128, c0:c0 + 512])
            for half in range(2):
                for pt in range(16):
                    dma(WA[:, pt * 512:(pt + 1) * 512], WoT[pt * 128:(pt + 1) * 128, half * 512:(half + 1) * 512])
                for tsb in range(4):
                    for pt in range(16):
                        mm(P0[:, :], BIG[:, pt * 512 + tsb * 128:pt * 512 + (tsb + 1) * 128], WA[:, pt * 512:(pt + 1) * 512], pt == 0, pt == 15)
                    t0 = c0 + tsb * 128
                    xt = SM[:, 0:512]
                    dma(xt, x[t0:t0 + 128, half * 512:(half + 1) * 512])
                    lt = SM[:, 512:1024]
                    dma(lt, r_ln1[t0:t0 + 128, half * 512:(half + 1) * 512])
                    rt = SM[:, 1024:1536]
                    tt(rt, P0[:, :], xt, OP.add)
                    tt(rt, rt, lt, OP.add)
                    dma(x1[t0:t0 + 128, half * 512:(half + 1) * 512], rt)

        for src, dst, n in ((wgi, wgf, 8), (wui, wuf, 8)):
            for kt in range(n):
                dma(SMI[:, 0:3072], src[kt * 128:(kt + 1) * 128, :])
                cpy(SM[:, 0:3072], SMI[:, 0:3072])
                ts(SM[:, 0:3072], SM[:, 0:3072], 1.0 / 127.0, OP.mult)
                dma(dst[kt * 128:(kt + 1) * 128, :], SM[:, 0:3072])
        for ht in range(24):
            dma(SMI[:, 0:1024], wdi[ht * 128:(ht + 1) * 128, :])
            cpy(SM[:, 0:1024], SMI[:, 0:1024])
            ts(SM[:, 0:1024], SM[:, 0:1024], 1.0 / 127.0, OP.mult)
            dma(wdf[ht * 128:(ht + 1) * 128, :], SM[:, 0:1024])

        def ffn(src, dst):
            HC = 4096
            for slab in range(8):
                c0 = slab * 512
                layer_norm_pass(src, 0, slab, False, None)
                for ht in range(24):
                    for kt in range(8):
                        dma(WA[:, 0:128], wgf[kt * 128:(kt + 1) * 128, ht * 128:(ht + 1) * 128])
                        mm(P0[:, :], WA[:, 0:128], BIG[:, kt * 512:(kt + 1) * 512], kt == 0, kt == 7)
                        dma(WA[:, 128:256], wuf[kt * 128:(kt + 1) * 128, ht * 128:(ht + 1) * 128])
                        mm(P1[:, :], WA[:, 128:256], BIG[:, kt * 512:(kt + 1) * 512], kt == 0, kt == 7)
                    gs = SM[:, 0:512]
                    act(gs, P0[:, :], AF.Silu)
                    tt(BIG[:, HC + ht * 512:HC + (ht + 1) * 512], gs, P1[:, :], OP.mult)
                for ht in range(24):
                    dma(WA[:, 0:1024], wdf[ht * 128:(ht + 1) * 128, :])
                    for tsb in range(4):
                        for dh in range(2):
                            mm([P0, P1, P2, P3, P4, P5, P6, P7][tsb * 2 + dh][:, :],
                               BIG[:, HC + ht * 512 + tsb * 128:HC + ht * 512 + (tsb + 1) * 128],
                               WA[:, dh * 512:(dh + 1) * 512], ht == 0, ht == 23)
                for tsb in range(4):
                    t0 = c0 + tsb * 128
                    for dh in range(2):
                        xt = SM[:, 0:512]
                        dma(xt, src[t0:t0 + 128, dh * 512:(dh + 1) * 512])
                        rt = SM[:, 512:1024]
                        tt(rt, [P0, P1, P2, P3, P4, P5, P6, P7][tsb * 2 + dh][:, :], xt, OP.add)
                        dma(dst[t0:t0 + 128, dh * 512:(dh + 1) * 512], rt)

        ffn(x1, x2)

        for slab in range(8):
            c0 = slab * 512
            layer_norm_pass(x2, 0, slab, False, None)
            for ft in range(32):
                for kt in range(8):
                    dma(WA[:, 0:128], Wbin[kt * 128:(kt + 1) * 128, ft * 128:(ft + 1) * 128])
                    mm(P0[:, :], WA[:, 0:128], BIG[:, kt * 512:(kt + 1) * 512], kt == 0, kt == 7)
                zt = SM[:, 0:512]
                cpy(zt, P0[:, :])
                if ft < 16:
                    dma(inpad[ft * 128:(ft + 1) * 128, c0 + 1:c0 + 513], zt)
                else:
                    dma(gate_b[(ft - 16) * 128:(ft - 15) * 128, c0:c0 + 512], zt)
        for ct in range(16):
            for slab in range(8):
                c0 = slab * 512
                xin = SM[:, 0:514]
                dma(xin, inpad[ct * 128:(ct + 1) * 128, c0:c0 + 514])
                accv = SM[:, 1024:1536]
                tmpv = SM[:, 1536:2048]
                ts(accv, xin[:, 0:512], cols[:, 133 + ct * 3:134 + ct * 3], OP.mult)
                for j in range(1, 3):
                    ts(tmpv, xin[:, j:j + 512], cols[:, 133 + ct * 3 + j:134 + ct * 3 + j], OP.mult)
                    tt(accv, accv, tmpv, OP.add)
                ts(accv, accv, cols[:, 117 + ct:118 + ct], OP.add)
                act(accv, accv, AF.Silu)
                dma(hb_cm[ct * 128:(ct + 1) * 128, c0:c0 + 512], accv)
        for slab in range(8):
            c0 = slab * 512
            for it in range(16):
                dma(BIG[:, it * 512:(it + 1) * 512], hb_cm[it * 128:(it + 1) * 128, c0:c0 + 512])
                dma(WA[:, it * 32:(it + 1) * 32], Wst[it * 128:(it + 1) * 128, :])
                mm(P1[0:32, :], WA[:, it * 32:(it + 1) * 32], BIG[:, it * 512:(it + 1) * 512], it == 0, it == 15)
            cpy(SM[0:32, 0:512], P1[0:32, :])
            dma(Bcm2[:, c0:c0 + 512], SM[16:32, 0:512])
            for ot in range(16):
                for it in range(16):
                    dma(WA[:, 2048 + 0:2048 + 128], Wtm[it * 128:(it + 1) * 128, ot * 128:(ot + 1) * 128])
                    mm(P2[:, :], WA[:, 2048:2048 + 128], BIG[:, it * 512:(it + 1) * 512], it == 0, it == 15)
                dv = SM[:, 0:512]
                ts(dv, P2[:, :], cols[:, 181 + ot:182 + ot], OP.add)
                softplus(dv, dv, SM[:, 512:1024], SM[:, 1024:1536])
                dma(dtb_cm[ot * 128:(ot + 1) * 128, c0:c0 + 512], dv)
        DTA = 0
        BR2 = 12288
        GA = 2048
        BR = 4096
        YA = 6144
        HA = 8192
        TMPA = 10240
        TMPB = 10496
        T0A = 10752
        GTA = 11008
        A2sb = U[:, 256:512]
        dma(A2sb, Ab2[:, :])
        act(A2sb, A2sb, AF.Exp)
        act(A2sb, A2sb, AF.Copy, scale=-1.0)
        mset(Sb_sb[:, :], 0.0)
        for cq in range(32):
            c0 = cq * 128
            for gi in range(16):
                dma(BIG[:, DTA + gi * 128:DTA + (gi + 1) * 128], dtb_cm[gi * 128:(gi + 1) * 128, c0:c0 + 128])
                dma(BIG[:, HA + gi * 128:HA + (gi + 1) * 128], hb_cm[gi * 128:(gi + 1) * 128, c0:c0 + 128])
                tt(BIG[:, GA + gi * 128:GA + (gi + 1) * 128], BIG[:, DTA + gi * 128:DTA + (gi + 1) * 128], BIG[:, HA + gi * 128:HA + (gi + 1) * 128], OP.mult)
            dma(BIG[0:1, BR2:BR2 + 2048].rearrange("p (s t) -> p s t", s=16),
                Bcm2[:, c0:c0 + 128].rearrange("s t -> () s t"))
            for qq in range(4):
                mm(P3[:, :], ones1r, BIG[0:1, BR2 + qq * 512:BR2 + (qq + 1) * 512], True, True)
                cpy(BIG[:, BR + qq * 512:BR + (qq + 1) * 512], P3[:, :])
            a2v = A2sb.rearrange("p (s g) -> p s g", s=16)
            for t in range(128):
                dtv = BIG[:, DTA:DTA + 2048].rearrange("p (g t) -> p t g", g=16)[:, t:t + 1, :].broadcast_to([128, 16, 16])
                gv = BIG[:, GA:GA + 2048].rearrange("p (g t) -> p t g", g=16)[:, t:t + 1, :].broadcast_to([128, 16, 16])
                bv = BIG[:, BR:BR + 2048].rearrange("p (s t) -> p s t", s=16)[:, :, t:t + 1].broadcast_to([128, 16, 16])
                tmpa = BIG[:, TMPA:TMPA + 256].rearrange("p (s g) -> p s g", s=16)
                tmpb = BIG[:, TMPB:TMPB + 256].rearrange("p (s g) -> p s g", s=16)
                t0v = BIG[:, T0A:T0A + 256]
                tt(tmpa, dtv, a2v, OP.mult)
                act(BIG[:, TMPA:TMPA + 256], BIG[:, TMPA:TMPA + 256], AF.Exp)
                tt(t0v, Sb_sb[:, :], BIG[:, TMPA:TMPA + 256], OP.mult)
                tt(tmpb, gv, bv, OP.mult)
                tt(Sb_sb[:, :], t0v, BIG[:, TMPB:TMPB + 256], OP.add)
                red(BIG[:, YA:YA + 2048].rearrange("p (g t) -> p g t", g=16)[:, :, t:t + 1],
                    Sb_sb.rearrange("p (s g) -> p g s", s=16), OP.add)
            for gi in range(16):
                dma(BIG[:, GTA + gi * 128:GTA + (gi + 1) * 128], gate_b[gi * 128:(gi + 1) * 128, c0:c0 + 128])
                act(BIG[:, GTA + gi * 128:GTA + (gi + 1) * 128], BIG[:, GTA + gi * 128:GTA + (gi + 1) * 128], AF.Silu)
                ts(BIG[:, TMPA:TMPA + 128] if False else BIG[:, HA + gi * 128:HA + (gi + 1) * 128],
                   BIG[:, HA + gi * 128:HA + (gi + 1) * 128], cols[:, 197 + gi:198 + gi], OP.mult)
                tt(BIG[:, YA + gi * 128:YA + (gi + 1) * 128], BIG[:, YA + gi * 128:YA + (gi + 1) * 128], BIG[:, HA + gi * 128:HA + (gi + 1) * 128], OP.add)
                tt(BIG[:, YA + gi * 128:YA + (gi + 1) * 128], BIG[:, YA + gi * 128:YA + (gi + 1) * 128], BIG[:, GTA + gi * 128:GTA + (gi + 1) * 128], OP.mult)
                dma(ysg_cm[gi * 128:(gi + 1) * 128, c0:c0 + 128], BIG[:, YA + gi * 128:YA + (gi + 1) * 128])
        for slab in range(8):
            c0 = slab * 512
            for pt in range(16):
                dma(BIG[:, pt * 512:(pt + 1) * 512], ysg_cm[pt * 128:(pt + 1) * 128, c0:c0 + 512])
            for half in range(2):
                for pt in range(16):
                    dma(WA[:, pt * 512:(pt + 1) * 512], WbO[pt * 128:(pt + 1) * 128, half * 512:(half + 1) * 512])
                for tsb in range(4):
                    for pt in range(16):
                        mm(P0[:, :], BIG[:, pt * 512 + tsb * 128:pt * 512 + (tsb + 1) * 128], WA[:, pt * 512:(pt + 1) * 512], pt == 0, pt == 15)
                    t0 = c0 + tsb * 128
                    xt = SM[:, 0:512]
                    dma(xt, x2[t0:t0 + 128, half * 512:(half + 1) * 512])
                    rt = SM[:, 1024:1536]
                    tt(rt, P0[:, :], xt, OP.add)
                    dma(x3[t0:t0 + 128, half * 512:(half + 1) * 512], rt)

        ffn(x3, out)

        def mk(eng_label):
            def f(e):
                for (en, wv, inc, fn) in sched:
                    if en == eng_label:
                        if wv >= 0:
                            e.wait_ge(g, wv)
                        fn(e).then_inc(g, inc)
                if eng_label == "sync":
                    e.wait_ge(g, ctr[0])
            return f
        blk.sync(mk("sync"))
        blk.tensor(mk("tensor"))
        blk.vector(mk("vector"))
        blk.scalar(mk("scalar"))
    return nc


def kernel(**inputs):
    i = inputs
    xall = np.ascontiguousarray(np.asarray(i["L0_attn_a_x"], np.float32))
    WaT = np.zeros((1024, 4352), np.float32)
    WaT[:, :4256] = np.asarray(i["L0_attn_a_in_proj_w"], np.float32).T
    acw = np.ascontiguousarray(np.asarray(i["L0_attn_a_conv_w"], np.float32).reshape(2176, 4).reshape(17, 128, 4).transpose(1, 0, 2).reshape(128, 68))
    acb = np.ascontiguousarray(np.asarray(i["L0_attn_a_conv_b"], np.float32).reshape(17, 128).T)
    dtbc = np.ascontiguousarray(np.asarray(i["L0_attn_a_dt_bias"], np.float32).reshape(32, 1))
    Aac = np.ascontiguousarray(np.asarray(i["L0_attn_a_A_log"], np.float32).reshape(32, 1))
    Dsk = np.ascontiguousarray(np.repeat(np.asarray(i["L0_attn_a_D_skip"], np.float32), 64).reshape(16, 128).T)
    nrm = np.ascontiguousarray(np.asarray(i["L0_attn_a_norm_w"], np.float32).reshape(16, 128).T)
    WoT = np.ascontiguousarray(np.asarray(i["L0_attn_a_out_proj_w"], np.float32).T)
    lnw = np.ascontiguousarray(np.tile(np.asarray(i["L0_attn_a_ln_w"], np.float32), (128, 1)))
    lnb = np.ascontiguousarray(np.tile(np.asarray(i["L0_attn_a_ln_b"], np.float32), (128, 1)))
    wgi = np.ascontiguousarray(np.asarray(i["L0_ffn_w_gate"], np.int32).T)
    wui = np.ascontiguousarray(np.asarray(i["L0_ffn_w_up"], np.int32).T)
    wdi = np.ascontiguousarray(np.asarray(i["L0_ffn_w_down"], np.int32).T)
    Wbin = np.ascontiguousarray(np.asarray(i["L0_attn_b_W_in"], np.float32))
    bcw = np.ascontiguousarray(np.asarray(i["L0_attn_b_conv_w"], np.float32).reshape(16, 128, 3).transpose(1, 0, 2).reshape(128, 48))
    bcb = np.ascontiguousarray(np.asarray(i["L0_attn_b_conv_b"], np.float32).reshape(16, 128).T)
    Wst = np.ascontiguousarray(np.asarray(i["L0_attn_b_W_state"], np.float32))
    Wtm = np.ascontiguousarray(np.asarray(i["L0_attn_b_W_time"], np.float32))
    btm = np.ascontiguousarray(np.asarray(i["L0_attn_b_b_time"], np.float32).reshape(16, 128).T)
    Ab2 = np.ascontiguousarray(np.asarray(i["L0_attn_b_A_log"], np.float32).reshape(16, 128, 16).transpose(1, 2, 0).reshape(128, 256))
    skipc = np.ascontiguousarray(np.asarray(i["L0_attn_b_skip"], np.float32).reshape(16, 128).T)
    WbO = np.ascontiguousarray(np.asarray(i["L0_attn_b_W_out"], np.float32))
    id128 = np.eye(128, dtype=np.float32)
    triu = np.triu(np.ones((128, 128), np.float32))
    nc = build()
    shared = dict(WaT=WaT, acw=acw, acb=acb, dtbc=dtbc, Aac=Aac, Dsk=Dsk, nrm=nrm,
                  WoT=WoT, lnw=lnw, lnb=lnb, wgi=wgi, wui=wui, wdi=wdi, Wbin=Wbin,
                  bcw=bcw, bcb=bcb, Wst=Wst, Wtm=Wtm, btm=btm, Ab2=Ab2, skipc=skipc,
                  WbO=WbO, id128=id128, triu=triu)
    in_maps = []
    for c in range(8):
        m = dict(shared)
        m["x"] = np.ascontiguousarray(xall[c % 4])
        in_maps.append(m)
    res = run_bass_kernel_spmd(nc, in_maps, core_ids=list(range(8)))
    outs = [res.results[c]["out"] for c in range(4)]
    return np.stack(outs, 0).astype(np.float32)
